# version 05
# baseline (speedup 1.0000x reference)
import numpy as np
import ml_dtypes
from contextlib import ExitStack

import concourse.bass as bass
import concourse.mybir as mybir
from concourse.bass_utils import run_bass_kernel_spmd

F32 = mybir.dt.float32
BF16 = mybir.dt.bfloat16
I32 = mybir.dt.int32
U32 = mybir.dt.uint32
AF = mybir.ActivationFunctionType
ALU = mybir.AluOpType
AX = mybir.AxisListType

NCORES = 8
B_CORE = 2
L = 2048
D = 1024
T = B_CORE * L
NT = T // 128
H, HD, C = 16, 64, 64
NC = L // C
NE = 32
HID = 512
CAP = 448
EST = 512
DUMP = NE * EST
NSLOT = NE * EST + 64
NPAIR = 2 * NT

ENGINES = ("pe", "act", "dve", "pool", "sp")
QUEUES = {"qg": "pool", "qs": "sp", "qa": "act"}
QLANES = {"qs": 6, "qg": 6, "qa": 4}
LANE_DOMS = tuple(f"{q}_{i}" for q in QUEUES for i in range(QLANES[q]))
DOMAINS = ENGINES + LANE_DOMS


class Op:
    __slots__ = ("engine", "emit", "reads", "writes", "queue", "eidx", "qidx",
                 "deps", "marked", "midx", "waits", "tag", "dom")

    def __init__(self, engine, emit, reads, writes, queue, tag):
        self.engine = engine
        self.emit = emit
        self.reads = tuple(reads)
        self.writes = tuple(writes)
        self.queue = queue
        self.tag = tag
        self.deps = []
        self.marked = False
        self.midx = None
        self.waits = []


class SB:

    def __init__(self, nc):
        self.nc = nc
        self.items = []
        self.bars = []

    def op(self, engine, emit, reads=(), writes=(), tag=""):
        o = Op(engine, emit, reads, writes, None, tag)
        self.items.append(o)
        return o

    def dma(self, queue, emit, reads=(), writes=(), tag=""):
        o = Op(QUEUES[queue], emit, reads, writes, queue, tag)
        self.items.append(o)
        return o

    def barrier(self, tag="bar"):
        o = Op(None, None, (), (), None, tag)
        self.items.append(o)
        return o

    def _analyze(self):
        last_w = {}
        readers = {}
        all_ops_by_dom = {d: [] for d in DOMAINS}
        qcount = {q: 0 for q in QUEUES}
        for o in self.items:
            if o.engine is None:
                for d in DOMAINS:
                    if all_ops_by_dom[d]:
                        o.deps.append(all_ops_by_dom[d][-1])
                last_w = {k: o for k in last_w}
                readers = {}
                continue
            if o.queue:
                lane = qcount[o.queue] % QLANES[o.queue]
                qcount[o.queue] += 1
                o.dom = f"{o.queue}_{lane}"
                prev = all_ops_by_dom[o.dom]
                if prev:
                    o.deps.append(prev[-1])
            else:
                o.dom = o.engine
            dom = o.dom
            for r in o.reads:
                if r in last_w:
                    o.deps.append(last_w[r])
            for w in o.writes:
                if w in last_w:
                    o.deps.append(last_w[w])
                for rd in readers.get(w, ()):
                    o.deps.append(rd)
            for r in o.reads:
                readers.setdefault(r, []).append(o)
            for w in o.writes:
                last_w[w] = o
                readers[w] = []
            all_ops_by_dom[dom].append(o)
            o.eidx = len(all_ops_by_dom[dom]) - 1
        for o in self.items:
            ded = []
            seen = set()
            for d in o.deps:
                if id(d) in seen or d is o:
                    continue
                seen.add(id(d))
                ded.append(d)
                if d.engine is not None:
                    d.marked = True
            o.deps = ded
        for o in self.items:
            if o.queue:
                o.marked = True
        mcount = {d: 0 for d in DOMAINS}
        for o in self.items:
            if o.engine is None:
                continue
            if o.marked:
                mcount[o.dom] += 1
                o.midx = mcount[o.dom]
        ZERO = {d: 0 for d in DOMAINS}
        know = {e: dict(ZERO) for e in ENGINES}
        cclk = {}
        iclk = {}

        def merge(dst, src):
            for k, v in src.items():
                if v > dst[k]:
                    dst[k] = v

        for o in self.items:
            if o.engine is None:
                target = dict(ZERO)
                for d in o.deps:
                    merge(target, cclk[id(d)])
                o.waits = target
                for e in ENGINES:
                    merge(know[e], target)
                cclk[id(o)] = dict(target)
                continue
            dom = o.dom
            eng = o.engine
            need = dict(ZERO)
            for d in o.deps:
                if d.engine is None:
                    merge(need, cclk[id(d)])
                    continue
                dd = d.dom
                if d.queue is None and dd == "pe" and eng == "pe" and o.queue is None:
                    merge(need, iclk[id(d)])
                else:
                    if d.midx > need[dd]:
                        need[dd] = d.midx
            K = know[eng]
            o.waits = [(k, v) for k, v in need.items() if v > K[k]]
            merge(K, need)
            for d in o.deps:
                if d.engine is None:
                    continue
                dd = d.dom
                if d.midx is not None and K[dd] >= d.midx:
                    merge(K, cclk[id(d)])
            iclk[id(o)] = dict(K)
            c = dict(K)
            if o.marked and o.midx > c.get(dom, 0):
                c[dom] = o.midx
            cclk[id(o)] = c

    def emit_into(self, block, sems):
        self._analyze()
        per_engine = {e: [] for e in ENGINES}
        for o in self.items:
            if o.engine is None:
                for e in ENGINES:
                    per_engine[e].append(o)
            else:
                per_engine[o.engine].append(o)

        def val(dom, v):
            return 16 * v if dom not in ENGINES else v

        def mkbody(ename):
            ops = per_engine[ename]

            def body(eng):
                for o in ops:
                    if o.engine is None:
                        for k, v in sorted(o.waits.items()):
                            if v > 0:
                                eng.wait_ge(sems[k], val(k, v))
                        continue
                    for k, v in o.waits:
                        eng.wait_ge(sems[k], val(k, v))
                    ins = o.emit(eng)
                    if o.queue:
                        ins.then_inc(sems[o.dom], 16)
                    elif o.marked:
                        ins.then_inc(sems[o.engine], 1)
            return body

        block.gpsimd(mkbody("pool"))
        block.vector(mkbody("dve"))
        block.scalar(mkbody("act"))
        block.tensor(mkbody("pe"))
        block.sync(mkbody("sp"))

    def totals(self):
        t = {d: 0 for d in DOMAINS}
        for o in self.items:
            if o.engine is None:
                continue
            if o.marked:
                t[o.dom] = max(t[o.dom], o.midx or 0)
        return t


def ap(t, offset, pattern):
    return bass.AP(t, offset, pattern)


def kernel(**inputs):
    x = np.ascontiguousarray(np.asarray(inputs["L0_attn_hidden_states"], np.float32))
    nc, names = _build()
    consts = _consts()
    in_maps = []
    for c in range(NCORES):
        m = {"x0": x[2 * c: 2 * c + 2]}
        for k, v in _weights(inputs).items():
            m[k] = v
        m.update(consts)
        in_maps.append(m)
    res = run_bass_kernel_spmd(nc, in_maps, core_ids=list(range(NCORES)))
    shards = [np.asarray(r["out"], np.float32) for r in res.results]
    return np.concatenate(shards, axis=0).reshape(16, 2048, 1024)


def _weights(inputs):
    out = {}
    for k, v in inputs.items():
        if k == "L0_attn_hidden_states":
            continue
        out[k.replace("L0_attn_", "a_").replace("L0_ffn_", "f_")] = (
            np.ascontiguousarray(np.asarray(v, np.float32)))
    return out


def _consts():
    ident = np.eye(128, dtype=np.float32)
    tri = (np.arange(64)[:, None] <= np.arange(64)[None, :]).astype(np.float32)
    tri2 = np.concatenate([tri, tri], axis=0)
    ustrict = (np.arange(128)[:, None] < np.arange(128)[None, :]).astype(np.float32)
    return {"c_ident": ident, "c_tri": tri2, "c_ustrict": ustrict}


def _build(upto="all"):
    nc = bass.Bass("TRN2", target_bir_lowering=False)
    sb = SB(nc)
    names = {}
    _emit_all(nc, sb, upto)
    return nc, names



def _emit_all(nc, sb, upto):
    st = ExitStack()
    g = {}

    def dram(name, shape, dtype, kind="Internal"):
        t = nc.dram_tensor(name, shape, dtype, kind=kind)
        g[name] = t.ap()
        return g[name]

    def sbuf(name, shape, dtype, ctx_=None):
        t = (ctx_ or st).enter_context(nc.sbuf_tensor(name, shape, dtype))
        g[name] = t
        return t

    def psum(name, shape, dtype, ctx_=None):
        t = (ctx_ or st).enter_context(nc.psum_tensor(name, shape, dtype))
        g[name] = t
        return t

    dram("x0", [B_CORE, L, D], F32, "ExternalInput")
    dram("out", [B_CORE, L, D], F32, "ExternalOutput")
    for nm, shape in [
        ("a_wq_w", [D, D]), ("a_wk_w", [D, D]), ("a_wv_w", [D, D]),
        ("a_wo_w", [D, D]), ("a_wq_b", [D]), ("a_wk_b", [D]),
        ("a_wv_b", [D]), ("a_wo_b", [D]),
        ("a_ttt_norm_weight", [H, HD]), ("a_ttt_norm_bias", [H, HD]),
        ("a_ttt_lr_weight", [H, 1, D]), ("a_ttt_lr_bias", [H, 1]),
        ("a_W1", [H, HD, HD]), ("a_b1", [H, 1, HD]),
        ("a_post_norm_w", [D]), ("a_post_norm_b", [D]),
        ("f_router_w0", [D, NE]), ("f_router_w1", [D, NE]),
        ("f_wg0", [NE, D, HID]), ("f_wg1", [NE, D, HID]),
        ("f_wu0", [NE, D, HID]), ("f_wu1", [NE, D, HID]),
        ("f_wd0", [NE, HID, D]), ("f_wd1", [NE, HID, D]),
        ("c_ident", [128, 128]), ("c_tri", [128, 64]),
        ("c_ustrict", [128, 128]),
    ]:
        dram(nm, shape, F32, "ExternalInput")

    dram("xr", [T, D], F32)
    dram("hgat0", [NSLOT, D], BF16)
    dram("hgat1", [NSLOT, D], BF16)
    dram("ybuf0", [NSLOT, D], BF16)
    dram("ybuf1", [NSLOT, D], BF16)
    dram("qtd", [T, D], BF16)
    dram("ktd", [T, D], BF16)
    dram("tgd", [T, D], BF16)
    dram("etad", [T, H], F32)
    for n in ("q", "k", "v", "o"):
        dram(f"wT_{n}", [128, 8 * D], BF16)

    sbuf("identB", [128, 128], BF16)
    sbuf("triF", [128, 64], F32)
    sbuf("ustrictB", [128, 128], BF16)
    sbuf("onesF", [1, 128], F32)
    sbuf("onesB", [1, 128], BF16)
    sbuf("routerB0", [128, 8, NE], BF16)
    sbuf("routerB1", [128, 8, NE], BF16)
    sbuf("Hbuf", [128, NT, D], BF16)
    for gi in range(2):
        sbuf(f"Ebuf{gi}", [128, NPAIR], F32)
        sbuf(f"Wbuf{gi}", [128, NPAIR], F32)
        sbuf(f"destI{gi}", [128, NPAIR], I32)
    sbuf("lrwT", [128, 8, H], BF16)
    for nm in ("r_tnw", "r_tnb"):
        sbuf(nm, [128, D], F32)
    sbuf("dumpC", [128, NPAIR], F32)
    sbuf("cb", [128, 8], F32)
    sbuf("onesCol", [128, 1], BF16)
    sbuf("flushb", [128, 64], F32)

    ctx = dict(nc=nc, sb=sb, g=g, st=st, sbuf=sbuf, psum=psum, dram=dram,
               taps=[])

    def stage(fn, name, *a):
        with ExitStack() as phase_ctx:
            ctx["phase"] = phase_ctx
            fn(ctx, *a)
        sb.barrier("after_" + name)
        return name == upto

    done = (
        stage(_p0_prep, "p0")
        or stage(_p1_route_in, "p1")
        or stage(_p2_routing, "p2", 0)
        or stage(_p2_routing, "p2b", 1)
        or stage(_p25_scatter, "p25", 0)
        or stage(_p3_experts, "p3", 0)
        or stage(_p4_combine_proj, "p4")
        or stage(_p5_scan, "p5")
        or stage(_p2_routing, "p2c", 0, 1)
        or stage(_p2_routing, "p2d", 1, 1)
        or stage(_p25_scatter, "p25b", 1)
        or stage(_p3_experts, "p3b", 1)
        or stage(_p8_output, "p8")
    )

    if upto != "all":
        _emit_taps(ctx, upto)
        sb.barrier("taps")

    with nc.Block() as block, ExitStack() as semctx:
        sems = {d: semctx.enter_context(nc.semaphore(f"s_{d}")) for d in DOMAINS}
        sb.emit_into(block, sems)
    st.close()


def _emit_taps(ctx, upto):
    sb, g, dram = ctx["sb"], ctx["g"], ctx["dram"]
    hb = dram("dbg_h", [128, NT * D], BF16, "ExternalOutput")
    sb.dma("qs", lambda e: e.dma_start(hb, ap(g["Hbuf"], 0, [[NT * D, 128], [1, NT * D]])),
           reads=("Hbuf",), tag="tap_h")
    for gi in range(2):
        eb = dram(f"dbg_e{gi}", [128, NPAIR], F32, "ExternalOutput")
        wb = dram(f"dbg_w{gi}", [128, NPAIR], F32, "ExternalOutput")
        sb.dma("qs", lambda e, eb=eb, gi=gi: e.dma_start(eb, g[f"Ebuf{gi}"][:, :]),
               reads=(f"Ebuf{gi}",), tag="tap_e")
        sb.dma("qs", lambda e, wb=wb, gi=gi: e.dma_start(wb, g[f"Wbuf{gi}"][:, :]),
               reads=(f"Wbuf{gi}",), tag="tap_w")
        db = dram(f"dbg_d{gi}", [128, NPAIR], I32, "ExternalOutput")
        sb.dma("qs", lambda e, db=db, gi=gi: e.dma_start(db, g[f"destI{gi}"][:, :]),
               reads=(f"destI{gi}",), tag="tap_d")
    if upto in ("p25", "p3", "p4", "p5", "p8", "p25b", "p3b"):
        for gi in range(2):
            hg = dram(f"dbg_hgat{gi}", [NSLOT, D], BF16, "ExternalOutput")
            sb.dma("qs", lambda e, hg=hg, gi=gi: e.dma_start(hg, g[f"hgat{gi}"]),
                   reads=(f"hgat{gi}",), tag="tap_hg")
    if upto in ("p3", "p4", "p5", "p8", "p3b"):
        for gi in range(2):
            yb = dram(f"dbg_ybuf{gi}", [NSLOT, D], BF16, "ExternalOutput")
            sb.dma("qs", lambda e, yb=yb, gi=gi: e.dma_start(yb, g[f"ybuf{gi}"]),
                   reads=(f"ybuf{gi}",), tag="tap_yb")
    if upto in ("p4", "p5", "p8"):
        xb = dram("dbg_xr", [T, D], F32, "ExternalOutput")
        sb.dma("qs", lambda e: e.dma_start(xb, g["xr"]), reads=("xr",), tag="tap_xr")
        for nm in ("qtd", "ktd", "tgd"):
            tb = dram(f"dbg_{nm}", [T, D], BF16, "ExternalOutput")
            sb.dma("qs", lambda e, tb=tb, nm=nm: e.dma_start(tb, g[nm]),
                   reads=(nm,), tag="tap_s")
        eb = dram("dbg_etad", [T, H], F32, "ExternalOutput")
        sb.dma("qs", lambda e: e.dma_start(eb, g["etad"]), reads=("etad",), tag="tap_eta")


def _build_repls(ctx, repls):
    nc, sb, g = ctx["nc"], ctx["sb"], ctx["g"]
    ph = ctx["phase"]
    sbuf, psum = ctx["sbuf"], ctx["psum"]
    n_ = g.get("_rstage_n", 0)
    g["_rstage_n"] = n_ + 1
    loc = ExitStack()
    rstage = loc.enter_context(nc.sbuf_tensor(f"rstage{n_}", [1, D], F32))
    psR = loc.enter_context(nc.psum_tensor(f"psR{n_}", [128, 512], F32))
    for nm, srcn, width in repls:
        srcap = g[srcn]
        flat = srcap.rearrange("a b -> (a b)") if len(srcap.shape) == 2 else srcap
        sb.dma("qs", lambda e, flat=flat, width=width: e.dma_start(
            rstage[0:1, 0:width], flat), writes=("rstage_raw",), tag="ld_" + nm)
        sb.dma("qs", lambda e: e.dma_start(
            g["flushb"][:, :], g["c_tri"]),
            reads=("rstage_raw",), writes=("rstage", "flushb"), tag="fence_" + nm)
        for half in range(max(1, width // 512)):
            w0 = half * 512
            w1 = min(width, w0 + 512)
            sb.op("pe", lambda e, w0=w0, w1=w1: e.matmul(
                psR[:, 0:w1 - w0], g["onesF"][:, :], rstage[0:1, w0:w1],
                start=True, stop=True),
                reads=("rstage", "rstage_raw", "onesF"), writes=("psR",),
                tag="mm_" + nm)
            sb.op("dve", lambda e, nm=nm, w0=w0, w1=w1: e.tensor_copy(
                g[nm][:, w0:w1], psR[:, 0:w1 - w0]),
                reads=("psR",), writes=(nm,), tag="cp_" + nm)
    loc.close()
    sb.barrier("replbar")


def _p0_prep(ctx):
    nc, sb, g = ctx["nc"], ctx["sb"], ctx["g"]
    ph = ctx["phase"]
    sbuf, psum = ctx["sbuf"], ctx["psum"]

    sb.dma("qg", lambda e: e.dma_start(g["identB"][:, :], g["c_ident"]),
           writes=("identB",), tag="ld_ident")
    sb.dma("qs", lambda e: e.dma_start(g["triF"][:, :], g["c_tri"]),
           writes=("triF",), tag="ld_tri")
    sb.dma("qg", lambda e: e.dma_start(g["ustrictB"][:, :], g["c_ustrict"]),
           writes=("ustrictB",), tag="ld_ustrict")
    sb.op("dve", lambda e: e.memset(g["onesF"][:, :], 1.0), writes=("onesF",))
    sb.op("dve", lambda e: e.memset(g["onesB"][:, :], 1.0), writes=("onesB",))
    sb.op("dve", lambda e: e.memset(g["onesCol"][:, :], 1.0), writes=("onesCol",))
    sb.op("dve", lambda e: e.memset(g["dumpC"][:, :], float(DUMP)), writes=("dumpC",))
    for col, v in enumerate([1e-5, 1e-6, 0.0, 1.0]):
        sb.op("dve", lambda e, col=col, v=v: e.memset(g["cb"][:, col:col + 1], v),
              writes=("cb",))
    for gi in range(2):
        sb.dma("qg", lambda e, gi=gi: e.dma_start(
            g[f"routerB{gi}"][:, :, :],
            g[f"f_router_w{gi}"].rearrange("(kc p) e -> p kc e", p=128)),
            writes=(f"routerB{gi}",), tag="ld_router")

    _build_repls(ctx, [("r_tnw", "a_ttt_norm_weight", D),
                       ("r_tnb", "a_ttt_norm_bias", D)])

    wnat = sbuf("wnat", [128, D], BF16, ph)
    wTs = sbuf("wTs", [128, 8, D], BF16, ph)
    psT = psum("psT0", [128, 512], BF16, ph)
    for wn, src in (("q", "a_wq_w"), ("k", "a_wk_w"), ("v", "a_wv_w"),
                    ("o", "a_wo_w")):
        for jc in range(8):
            sb.dma("qg", lambda e, src=src, jc=jc: e.dma_start(
                wnat[:, :], g[src][jc * 128:(jc + 1) * 128, :]),
                writes=("wnat",), tag=f"ldw_{wn}{jc}")
            for kc in range(8):
                sb.op("pe", lambda e, kc=kc: e.transpose(
                    psT[:, 0:128],
                    wnat[:, kc * 128:(kc + 1) * 128], g["identB"][:, :]),
                    reads=("wnat", "identB"), writes=("psT0",),
                    tag=f"tr_{wn}{jc}{kc}")
                sb.op("dve", lambda e, kc=kc, jc=jc: e.tensor_copy(
                    wTs[:, kc, jc * 128:(jc + 1) * 128],
                    psT[:, 0:128]),
                    reads=("psT0",), writes=(f"wTs_{kc}",),
                    tag=f"ev_{wn}{jc}{kc}")
        sb.dma("qs", lambda e, wn=wn: e.dma_start(
            g[f"wT_{wn}"], ap(wTs, 0, [[8 * D, 128], [1, 8 * D]])),
            reads=tuple(f"wTs_{kc}" for kc in range(8)),
            writes=(f"wT_{wn}_d",), tag=f"st_{wn}")

    lrnat = sbuf("lrnat", [16, D], BF16, ph)
    sb.dma("qg", lambda e: e.dma_start(
        lrnat[:, :], g["a_ttt_lr_weight"].rearrange("h o d -> (h o) d")),
        writes=("lrnat_raw",), tag="ld_lrnat")
    sb.dma("qg", lambda e: e.dma_start(
        g["flushb"][:, :], g["c_tri"]),
        reads=("lrnat_raw",), writes=("lrnat", "flushb"), tag="fence_lrnat")
    for kc in range(8):
        sb.op("pe", lambda e, kc=kc: e.transpose(
            psT[:, 0:16], lrnat[:, kc * 128:(kc + 1) * 128],
            g["identB"][0:16, 0:16]),
            reads=("lrnat", "lrnat_raw", "identB"), writes=("psT0",),
            tag=f"tr_lrw{kc}")
        sb.op("dve", lambda e, kc=kc: e.tensor_copy(
            g["lrwT"][:, kc, :], psT[:, 0:16]),
            reads=("psT0",), writes=("lrwT",), tag=f"ev_lrw{kc}")


def _p1_route_in(ctx):
    nc, sb, g = ctx["nc"], ctx["sb"], ctx["g"]
    ph = ctx["phase"]
    sbuf, psum = ctx["sbuf"], ctx["psum"]

    xt = [sbuf(f"xt{p}", [128, D], F32, ph) for p in range(2)]
    sqscr = [sbuf(f"sqscr{p}", [128, D], F32, ph) for p in range(2)]
    sm = [sbuf(f"p1sm{p}", [128, 16], F32, ph) for p in range(2)]
    hf = [sbuf(f"hf{p}", [128, 8, 128], BF16, ph) for p in range(2)]
    exps = [sbuf(f"exps{p}", [128, 2 * NE], F32, ph) for p in range(2)]
    scores = [sbuf(f"scores{p}", [128, 2 * NE], F32, ph) for p in range(2)]
    m8 = [sbuf(f"m8{p}", [128, 16], F32, ph) for p in range(2)]
    idx8 = [sbuf(f"idx8{p}", [128, 16], U32, ph) for p in range(2)]
    psTT = [psum("psTA", [128, 128], BF16, ph), psum("psTB", [128, 128], BF16, ph)]
    psL = [psum(f"psL{p}", [128, 2 * NE], F32, ph) for p in range(2)]

    x0f = g["x0"].rearrange("b l d -> (b l) d")
    for i in range(NT):
        p = i % 2
        S = sm[p]
        sb.dma("qs", lambda e, i=i, p=p: e.dma_start(
            xt[p][:, :], x0f[i * 128:(i + 1) * 128, :]),
            writes=(f"xt{p}",), tag=f"p1ld{i}")
        sb.op("dve", lambda e, p=p: e.reduce_sum(
            sm[p][:, 0:1], xt[p][:, :], axis=AX.X, negate=True),
            reads=(f"xt{p}",), writes=(f"p1sm{p}",), tag=f"p1s1_{i}")
        sb.op("act", lambda e, p=p: e.activation(
            sqscr[p][:, :], xt[p][:, :], AF.Square, accum_out=sm[p][:, 1:2]),
            reads=(f"xt{p}",), writes=(f"sqscr{p}", f"p1sm{p}b"), tag=f"p1sq_{i}")
        sb.op("dve", lambda e, p=p: e.tensor_scalar(
            sm[p][:, 2:3], sm[p][:, 0:1], 1.0 / D, None, ALU.mult),
            reads=(f"p1sm{p}",), writes=(f"p1sm{p}",), tag=f"p1mu_{i}")
        sb.op("dve", lambda e, p=p: e.tensor_scalar(
            sm[p][:, 3:4], sm[p][:, 1:2], 1.0 / D, None, ALU.mult),
            reads=(f"p1sm{p}b", f"p1sm{p}"), writes=(f"p1sm{p}",), tag=f"p1e2_{i}")
        sb.op("dve", lambda e, p=p: e.tensor_scalar(
            sm[p][:, 4:5], sm[p][:, 2:3], sm[p][:, 2:3], None, ALU.mult),
            reads=(f"p1sm{p}",), writes=(f"p1sm{p}",), tag=f"p1m2_{i}")
        sb.op("dve", lambda e, p=p: e.tensor_sub(
            sm[p][:, 5:6], sm[p][:, 3:4], sm[p][:, 4:5]),
            reads=(f"p1sm{p}",), writes=(f"p1sm{p}",), tag=f"p1var_{i}")
        sb.op("act", lambda e, p=p: e.activation(
            sm[p][:, 6:7], sm[p][:, 5:6], AF.Sqrt, bias=g["cb"][:, 0:1]),
            reads=(f"p1sm{p}", "cb"), writes=(f"p1sm{p}c",), tag=f"p1std_{i}")
        sb.op("dve", lambda e, p=p: e.reciprocal(
            sm[p][:, 7:8], sm[p][:, 6:7]),
            reads=(f"p1sm{p}c",), writes=(f"p1sm{p}",), tag=f"p1rstd_{i}")
        sb.op("dve", lambda e, p=p, i=i: e.tensor_scalar(
            g["Hbuf"][:, i, :], xt[p][:, :], sm[p][:, 2:3], sm[p][:, 7:8],
            ALU.add, ALU.mult),
            reads=(f"xt{p}", f"p1sm{p}"), writes=("Hbuf",), tag=f"p1h_{i}")
        for kc in range(8):
            bk = kc % 2
            sb.op("pe", lambda e, i=i, kc=kc, bk=bk: e.transpose(
                psTT[bk][:, :], g["Hbuf"][:, i, kc * 128:(kc + 1) * 128],
                g["identB"][:, :]),
                reads=("Hbuf", "identB"), writes=(f"psT{bk}",),
                tag=f"p1tr{i}_{kc}")
            sb.op("dve", lambda e, p=p, kc=kc, bk=bk: e.tensor_copy(
                hf[p][:, kc, :], psTT[bk][:, :]),
                reads=(f"psT{bk}",), writes=(f"hf{p}_{kc}",),
                tag=f"p1ev{i}_{kc}")
        for gi in range(2):
            for kc in range(8):
                sb.op("pe", lambda e, p=p, gi=gi, kc=kc: e.matmul(
                    psL[p][:, gi * NE:(gi + 1) * NE], hf[p][:, kc, :],
                    g[f"routerB{gi}"][:, kc, :],
                    start=(kc == 0), stop=(kc == 7)),
                    reads=(f"hf{p}_{kc}", f"routerB{gi}"),
                    writes=(f"psL{p}",), tag=f"p1mm{i}_{gi}{kc}")
        for gi in range(2):
            sb.op("dve", lambda e, p=p, gi=gi: e.reduce_max(
                sm[p][:, 8 + gi:9 + gi], psL[p][:, gi * NE:(gi + 1) * NE],
                axis=AX.X, negate=True),
                reads=(f"psL{p}",), writes=(f"p1sm{p}",), tag=f"p1nm{i}_{gi}")
            sb.op("act", lambda e, p=p, gi=gi: e.activation(
                exps[p][:, gi * NE:(gi + 1) * NE], psL[p][:, gi * NE:(gi + 1) * NE],
                AF.Exp, bias=sm[p][:, 8 + gi:9 + gi],
                accum_out=sm[p][:, 10 + gi:11 + gi]),
                reads=(f"psL{p}", f"p1sm{p}"),
                writes=(f"exps{p}", f"p1sm{p}d"), tag=f"p1exp{i}_{gi}")
            sb.op("dve", lambda e, p=p, gi=gi: e.reciprocal(
                sm[p][:, 12 + gi:13 + gi], sm[p][:, 10 + gi:11 + gi]),
                reads=(f"p1sm{p}d",), writes=(f"p1sm{p}",), tag=f"p1rc{i}_{gi}")
            sb.op("dve", lambda e, p=p, gi=gi: e.tensor_scalar(
                scores[p][:, gi * NE:(gi + 1) * NE], exps[p][:, gi * NE:(gi + 1) * NE],
                sm[p][:, 12 + gi:13 + gi], None, ALU.mult),
                reads=(f"exps{p}", f"p1sm{p}"), writes=(f"scores{p}",),
                tag=f"p1sc{i}_{gi}")
            sb.op("dve", lambda e, p=p, gi=gi: e.max(
                out=m8[p][:, gi * 8:(gi + 1) * 8],
                in_=scores[p][:, gi * NE:(gi + 1) * NE]),
                reads=(f"scores{p}",), writes=(f"m8{p}",), tag=f"p1m8{i}_{gi}")
            sb.op("dve", lambda e, p=p, gi=gi: e.max_index(
                out=idx8[p][:, gi * 8:(gi + 1) * 8],
                in_max=m8[p][:, gi * 8:(gi + 1) * 8],
                in_values=scores[p][:, gi * NE:(gi + 1) * NE]),
                reads=(f"scores{p}", f"m8{p}"), writes=(f"idx8{p}",),
                tag=f"p1mi{i}_{gi}")
            sb.op("dve", lambda e, p=p, gi=gi, i=i: e.tensor_copy(
                g[f"Ebuf{gi}"][:, 2 * i:2 * i + 2], idx8[p][:, gi * 8:gi * 8 + 2]),
                reads=(f"idx8{p}",), writes=(f"Ebuf{gi}",), tag=f"p1ec{i}_{gi}")
            sb.op("dve", lambda e, p=p, gi=gi, i=i: e.tensor_scalar(
                g[f"Wbuf{gi}"][:, 2 * i:2 * i + 2], m8[p][:, gi * 8:gi * 8 + 2],
                0.25, None, ALU.mult),
                reads=(f"m8{p}",), writes=(f"Wbuf{gi}",), tag=f"p1wc{i}_{gi}")


def _p2_routing(ctx, grp, call=0):
    nc, sb, g = ctx["nc"], ctx["sb"], ctx["g"]
    ph = ctx["phase"]
    sbuf, psum = ctx["sbuf"], ctx["psum"]
    gn = f"{grp}_{call}"

    iotaE = sbuf(f"iotaE{gn}", [128, NPAIR, NE], F32, ph)
    sb.op("pool", lambda e: e.iota(iotaE[:, :, :], [[0, NPAIR], [1, NE]],
                                   channel_multiplier=0,
                                   allow_small_or_imprecise_dtypes=True),
          writes=("iotaE",), tag="iotaE")
    Ip = sbuf(f"Ip{gn}", [128, NPAIR * NE], F32, ph)
    Tp = sbuf(f"Tp{gn}", [128, NPAIR * NE], F32, ph)
    ones = sbuf(f"onescan{gn}", [128, NPAIR], F32, ph)
    cnt16 = sbuf(f"cnt16{gn}", [128, NE], BF16, ph)
    baseF = sbuf(f"baseF{gn}", [128, NE], F32, ph)
    rank1 = sbuf(f"rank1{gn}", [128, NPAIR], F32, ph)
    mask = sbuf(f"mask{gn}", [128, NPAIR], U32, ph)
    dest = sbuf(f"dest{gn}", [128, NPAIR], F32, ph)
    psB = psum(f"psB{gn}", [128, NE], F32, ph)

    E = g[f"Ebuf{grp}"]
    sb.op("dve", lambda e: e.memset(ones[:, :], 1.0), writes=("p2ones",))
    sb.op("dve", lambda e: e.tensor_tensor(
        ap(Ip, 0, [[NPAIR * NE, 128], [NE, NPAIR], [1, NE]]),
        ap(E, 0, [[NPAIR, 128], [1, NPAIR], [0, NE]]),
        iotaE[:, :, :], ALU.is_equal),
        reads=(f"Ebuf{grp}", "iotaE"), writes=("p2Ip",), tag=f"p2eq{gn}")
    for ei in range(NE):
        sb.op("dve", lambda e, ei=ei: e.tensor_tensor_scan(
            ap(Tp, ei, [[NPAIR * NE, 128], [NE, NPAIR]]),
            ap(Ip, ei, [[NPAIR * NE, 128], [NE, NPAIR]]),
            ones[:, :], 0.0, ALU.add, ALU.mult),
            reads=("p2Ip", "p2ones"), writes=("p2Tp",), tag=f"p2sc{gn}_{ei}")
    sb.op("dve", lambda e: e.tensor_copy(
        cnt16[:, :], Tp[:, (NPAIR - 1) * NE:]),
        reads=("p2Tp",), writes=("p2cnt",), tag=f"p2cnt{gn}")
    sb.op("pe", lambda e: e.matmul(
        psB[:, :], g["ustrictB"][:, :], cnt16[:, :], start=True, stop=True),
        reads=("p2cnt", "ustrictB"), writes=("p2psB",), tag=f"p2base{gn}")
    sb.op("dve", lambda e: e.tensor_copy(baseF[:, :], psB[:, :]),
          reads=("p2psB",), writes=("p2base",), tag=f"p2bev{gn}")
    sb.op("dve", lambda e: e.tensor_tensor(
        ap(Tp, 0, [[NPAIR * NE, 128], [NE, NPAIR], [1, NE]]),
        ap(Tp, 0, [[NPAIR * NE, 128], [NE, NPAIR], [1, NE]]),
        ap(baseF, 0, [[NE, 128], [0, NPAIR], [1, NE]]), ALU.add),
        reads=("p2Tp", "p2base"), writes=("p2Tp",), tag=f"p2add{gn}")
    sb.op("dve", lambda e: e.tensor_mul(Tp[:, :], Tp[:, :], Ip[:, :]),
          reads=("p2Tp", "p2Ip"), writes=("p2Tp",), tag=f"p2mul{gn}")
    sb.op("dve", lambda e: e.reduce_sum(
        rank1[:, :], ap(Tp, 0, [[NPAIR * NE, 128], [NE, NPAIR], [1, NE]]),
        axis=AX.X),
        reads=("p2Tp",), writes=("p2rank",), tag=f"p2red{gn}")
    sb.op("dve", lambda e: e.tensor_scalar(
        rank1[:, :], rank1[:, :], 1.0, None, ALU.subtract),
        reads=("p2rank",), writes=("p2rank",), tag=f"p2rm{gn}")
    sb.op("dve", lambda e: e.scalar_tensor_tensor(
        dest[:, :], E[:, :], float(EST), rank1[:, :], ALU.mult, ALU.add),
        reads=(f"Ebuf{grp}", "p2rank"), writes=("p2dest",), tag=f"p2d0{gn}")
    sb.op("dve", lambda e: e.tensor_scalar(
        mask[:, :], rank1[:, :], float(CAP), None, ALU.is_ge),
        reads=("p2rank",), writes=("p2mask",), tag=f"p2ge{gn}")
    sb.op("dve", lambda e: e.select(
        dest[:, :], mask[:, :], g["dumpC"][:, :], dest[:, :]),
        reads=("p2mask", "dumpC", "p2dest"), writes=("p2dest",), tag=f"p2sel{gn}")
    sb.op("dve", lambda e: e.tensor_copy(g[f"destI{grp}"][:, :], dest[:, :]),
          reads=("p2dest",), writes=(f"destI{grp}",), tag=f"p2di{gn}")


def _p25_scatter(ctx, call):
    nc, sb, g = ctx["nc"], ctx["sb"], ctx["g"]
    ph = ctx["phase"]
    sbuf = ctx["sbuf"]
    zrow = sbuf(f"zrow{call}", [1, D], BF16, ph)
    sb.op("dve", lambda e: e.memset(zrow[:, :], 0.0), writes=("zrow",))
    for gi in range(2):
        hg = g[f"hgat{gi}"]
        sb.dma("qs", lambda e, gi=gi: e.dma_start(
            g[f"ybuf{gi}"][DUMP:DUMP + 1, :], zrow[:, :]),
            reads=("zrow",), writes=(f"ybufz{gi}",), tag=f"zdump{gi}")
        sb.dma("qs", lambda e, gi=gi: e.dma_start(
            g["flushb"][:, :], g["c_tri"]),
            reads=(f"ybufz{gi}",), writes=(f"ybuf{gi}", "flushb"),
            tag=f"zdumpf{gi}")
        for col in range(NPAIR):
            sb.dma("qg", lambda e, gi=gi, col=col, hg=hg: e.indirect_dma_start(
                hg, bass.IndirectOffsetOnAxis(g[f"destI{gi}"][:, col:col + 1], 0),
                g["Hbuf"][:, col // 2, :], None),
                reads=("Hbuf", f"destI{gi}"), writes=(f"hgat{gi}",),
                tag=f"hsc{gi}_{col}")


def _p3_experts(ctx, call):
    nc, sb, g = ctx["nc"], ctx["sb"], ctx["g"]
    ph = ctx["phase"]
    sbuf, psum = ctx["sbuf"], ctx["psum"]

    cs = f"c{call}"
    Wgb = [sbuf(f"p3wg{p}{cs}", [128, 8, HID], BF16, ph) for p in range(2)]
    Wub = [sbuf(f"p3wu{p}{cs}", [128, 8, HID], BF16, ph) for p in range(2)]
    Wdb = [sbuf(f"p3wd{p}{cs}", [128, 4, D], BF16, ph) for p in range(2)]
    xg = [sbuf(f"p3xg{p}{cs}", [128, 4, D], BF16, ph) for p in range(2)]
    Xf = [sbuf(f"p3xf{p}{cs}", [128, 8, CAP], BF16, ph) for p in range(2)]
    Hd = [sbuf(f"p3hd{p}{cs}", [128, 4, CAP], BF16, ph) for p in range(2)]
    Ysb = [sbuf(f"p3y{p}{cs}", [128, 4, D], BF16, ph) for p in range(2)]
    hsil = sbuf(f"p3sil{cs}", [128, CAP], F32, ph)
    wdF0 = sbuf(f"p3wdf{cs}", [128, 4, D], F32, ph)
    wdF = [wdF0, wdF0]
    psG = psum(f"p3psG{cs}", [128, CAP], F32, ph)
    psU = psum(f"p3psU{cs}", [128, CAP], F32, ph)
    psD = [psum(f"p3psD{p}{cs}", [128, D], F32, ph) for p in range(2)]
    psT = [psum(f"p3psTA{cs}", [128, 128], BF16, ph),
           psum(f"p3psTB{cs}", [128, 128], BF16, ph)]

    ROWS = [128, 128, 128, CAP - 384]
    for gi in range(2):
        for ei in range(NE):
            p = ei % 2
            wres = (f"p3w{p}",)
            sb.dma("qg", lambda e, gi=gi, ei=ei, p=p: e.dma_start(
                Wgb[p][:, :, :],
                g[f"f_wg{gi}"][ei].rearrange("(kc q) n -> q kc n", q=128)),
                writes=(f"p3wg{p}",), tag=f"wg{gi}_{ei}")
            sb.dma("qg", lambda e, gi=gi, ei=ei, p=p: e.dma_start(
                Wub[p][:, :, :],
                g[f"f_wu{gi}"][ei].rearrange("(kc q) n -> q kc n", q=128)),
                writes=(f"p3wu{p}",), tag=f"wu{gi}_{ei}")
            sb.dma("qs", lambda e, gi=gi, ei=ei, p=p: e.dma_start(
                wdF[p][:, :, :],
                g[f"f_wd{gi}"][ei].rearrange("(kc q) n -> q kc n", q=128)),
                writes=("p3wdf",), tag=f"wdl{gi}_{ei}")
            sb.op("dve", lambda e, p=p: e.tensor_copy(
                Wdb[p][:, :, :], wdF[p][:, :, :]),
                reads=("p3wdf",), writes=(f"p3wd{p}",), tag=f"wdc{gi}_{ei}")
            base = ei * EST
            sb.dma("qs", lambda e, gi=gi, base=base, p=p: e.dma_start(
                xg[p][:, :, :],
                g[f"hgat{gi}"][base:base + EST, :].rearrange(
                    "(t q) n -> q t n", q=128)),
                reads=(f"hgat{gi}",), writes=(f"p3xg{p}",), tag=f"xg{gi}_{ei}")
            for t in range(4):
                rows = ROWS[t]
                for kc in range(8):
                    bk = (t * 8 + kc) % 2
                    sb.op("pe", lambda e, p=p, t=t, kc=kc, bk=bk, rows=rows:
                          e.transpose(psT[bk][:, 0:rows],
                                      xg[p][0:rows, t, kc * 128:(kc + 1) * 128],
                                      g["identB"][0:rows, 0:rows]),
                          reads=(f"p3xg{p}", "identB"), writes=(f"p3psT{bk}",),
                          tag=f"p3tr{gi}_{ei}_{t}_{kc}")
                    sb.op("dve", lambda e, p=p, t=t, kc=kc, bk=bk, rows=rows:
                          e.tensor_copy(Xf[p][:, kc, t * 128:t * 128 + rows],
                                        psT[bk][:, 0:rows]),
                          reads=(f"p3psT{bk}",), writes=(f"p3xf{p}_{kc}",),
                          tag=f"p3ev{gi}_{ei}_{t}_{kc}")
            for m in range(4):
                for kc in range(8):
                    sb.op("pe", lambda e, p=p, m=m, kc=kc: e.matmul(
                        psG[:, :], Wgb[p][:, kc, m * 128:(m + 1) * 128],
                        Xf[p][:, kc, :], start=(kc == 0), stop=(kc == 7)),
                        reads=(f"p3wg{p}", f"p3xf{p}_{kc}"), writes=("p3psG",),
                        tag=f"p3g{gi}_{ei}_{m}_{kc}")
                for kc in range(8):
                    sb.op("pe", lambda e, p=p, m=m, kc=kc: e.matmul(
                        psU[:, :], Wub[p][:, kc, m * 128:(m + 1) * 128],
                        Xf[p][:, kc, :], start=(kc == 0), stop=(kc == 7)),
                        reads=(f"p3wu{p}", f"p3xf{p}_{kc}"), writes=("p3psU",),
                        tag=f"p3u{gi}_{ei}_{m}_{kc}")
                sb.op("act", lambda e: e.activation(
                    hsil[:, :], psG[:, :], AF.Silu),
                    reads=("p3psG",), writes=("p3sil",), tag=f"p3sil{gi}_{ei}_{m}")
                sb.op("dve", lambda e, p=p, m=m: e.tensor_mul(
                    Hd[p][:, m, :], hsil[:, :], psU[:, :]),
                    reads=("p3sil", "p3psU"), writes=(f"p3hd{p}_{m}",),
                    tag=f"p3hm{gi}_{ei}_{m}")
            for t in range(4):
                rows = ROWS[t]
                dp = t % 2
                for half in range(2):
                    for k2 in range(4):
                        sb.op("pe", lambda e, p=p, t=t, k2=k2, half=half, dp=dp,
                              rows=rows: e.matmul(
                            psD[dp][0:rows, half * 512:(half + 1) * 512],
                            Hd[p][:, k2, t * 128:t * 128 + rows],
                            Wdb[p][:, k2, half * 512:(half + 1) * 512],
                            start=(k2 == 0), stop=(k2 == 3)),
                            reads=(f"p3hd{p}_{k2}", f"p3wd{p}"),
                            writes=(f"p3psD{dp}",),
                            tag=f"p3d{gi}_{ei}_{t}_{half}_{k2}")
                sb.op("dve", lambda e, p=p, t=t, dp=dp, rows=rows: e.tensor_copy(
                    Ysb[p][0:rows, t, :], psD[dp][0:rows, :]),
                    reads=(f"p3psD{dp}",), writes=(f"p3y{p}",),
                    tag=f"p3ye{gi}_{ei}_{t}")
            sb.dma("qa", lambda e, gi=gi, base=base, p=p: e.dma_start(
                g[f"ybuf{gi}"][base:base + EST, :].rearrange(
                    "(t q) n -> q t n", q=128), Ysb[p][:, :, :]),
                reads=(f"p3y{p}",), writes=(f"ybuf{gi}",), tag=f"ysc{gi}_{ei}")


def _ln_tile(ctx, pfx, x_ap, res_x, out_ap, res_out, smb, res_sm, sqb, res_sq,
             epscol, i):
    sb, g = ctx["sb"], ctx["g"]
    sb.op("dve", lambda e: e.reduce_sum(smb[:, 0:1], x_ap, axis=AX.X, negate=True),
          reads=res_x, writes=(res_sm,), tag=f"{pfx}s1_{i}")
    sb.op("act", lambda e: e.activation(sqb, x_ap, AF.Square,
                                        accum_out=smb[:, 1:2]),
          reads=res_x, writes=(res_sq, res_sm + "b"), tag=f"{pfx}sq_{i}")
    sb.op("dve", lambda e: e.tensor_scalar(smb[:, 2:3], smb[:, 0:1], 1.0 / D,
                                           None, ALU.mult),
          reads=(res_sm,), writes=(res_sm,), tag=f"{pfx}mu_{i}")
    sb.op("dve", lambda e: e.tensor_scalar(smb[:, 3:4], smb[:, 1:2], 1.0 / D,
                                           None, ALU.mult),
          reads=(res_sm + "b", res_sm), writes=(res_sm,), tag=f"{pfx}e2_{i}")
    sb.op("dve", lambda e: e.tensor_scalar(smb[:, 4:5], smb[:, 2:3], smb[:, 2:3],
                                           None, ALU.mult),
          reads=(res_sm,), writes=(res_sm,), tag=f"{pfx}m2_{i}")
    sb.op("dve", lambda e: e.tensor_sub(smb[:, 5:6], smb[:, 3:4], smb[:, 4:5]),
          reads=(res_sm,), writes=(res_sm,), tag=f"{pfx}var_{i}")
    sb.op("act", lambda e: e.activation(smb[:, 6:7], smb[:, 5:6], AF.Sqrt,
                                        bias=g["cb"][:, epscol:epscol + 1]),
          reads=(res_sm, "cb"), writes=(res_sm + "c",), tag=f"{pfx}std_{i}")
    sb.op("dve", lambda e: e.reciprocal(smb[:, 7:8], smb[:, 6:7]),
          reads=(res_sm + "c",), writes=(res_sm,), tag=f"{pfx}rstd_{i}")
    sb.op("dve", lambda e: e.tensor_scalar(out_ap, x_ap, smb[:, 2:3], smb[:, 7:8],
                                           ALU.add, ALU.mult),
          reads=res_x + (res_sm,), writes=(res_out,), tag=f"{pfx}h_{i}")


def _p4_combine_proj(ctx):
    nc, sb, g = ctx["nc"], ctx["sb"], ctx["g"]
    ph = ctx["phase"]
    sbuf, psum = ctx["sbuf"], ctx["psum"]

    for nm in ("r_wqb", "r_wkb", "r_wvb"):
        sbuf(nm, [128, D], F32, ph)
    sbuf("r_lrb", [128, H], F32, ph)
    _build_repls(ctx, [("r_wqb", "a_wq_b", D), ("r_wkb", "a_wk_b", D),
                       ("r_wvb", "a_wv_b", D), ("r_lrb", "a_ttt_lr_bias", H)])
    WT = {}
    for wn in ("q", "k", "v"):
        WT[wn] = sbuf(f"p4wt{wn}", [128, 8, D], BF16, ph)
        sb.dma("qs", lambda e, wn=wn: e.dma_start(
            WT[wn][:, :, :], g[f"wT_{wn}"]),
            reads=(f"wT_{wn}_d",), writes=(f"p4wt{wn}",), tag=f"p4ldw{wn}")

    xt = [sbuf(f"p4x{p}", [128, D], F32, ph) for p in range(2)]
    yg = [sbuf(f"p4yg{j}", [128, D], BF16, ph) for j in range(4)]
    x1t = [sbuf(f"p4x1{p}", [128, D], F32, ph) for p in range(2)]
    hbt = [sbuf(f"p4hb{p}", [128, D], BF16, ph) for p in range(2)]
    smb = [sbuf(f"p4sm{p}", [128, 16], F32, ph) for p in range(2)]
    sqb0 = sbuf("p4sq", [128, D], F32, ph)
    sqb = [sqb0, sqb0]
    hBf = sbuf("p4hbf", [128, 8, 512], BF16, ph)
    psT = [psum("p4psTA", [128, 128], BF16, ph),
           psum("p4psTB", [128, 128], BF16, ph)]
    psP = [psum(f"p4psP{p}", [128, D], F32, ph) for p in range(2)]
    psLr = psum("p4psLr", [128, H], F32, ph)
    pf32 = sbuf("p4pf", [128, D], F32, ph)
    kn = sbuf("p4kn", [128, D], F32, ph)
    dts = sbuf("p4d", [128, D], F32, ph)
    t1 = sbuf("p4t1", [128, D], F32, ph)
    nbuf = sbuf("p4n", [128, 8 * H], F32, ph)
    qtb = sbuf("p4qtb", [128, D], BF16, ph)
    etb = sbuf("p4etb", [128, H], F32, ph)

    x0f = g["x0"].rearrange("b l d -> (b l) d")

    def seg_norm(pfx, srcb, res_src, ncol, i):
        sb.op("act", lambda e: e.activation(sqb[0][:, :], srcb, AF.Square),
              reads=(res_src,), writes=("p4sq",), tag=f"{pfx}sq{i}")
        sb.op("dve", lambda e: e.reduce_sum(
            nbuf[:, ncol:ncol + 16],
            ap(sqb[0], 0, [[D, 128], [64, 16], [1, 64]]), axis=AX.X),
            reads=("p4sq",), writes=("p4n",), tag=f"{pfx}red{i}")
        sb.op("act", lambda e: e.activation(
            nbuf[:, ncol + 16:ncol + 32], nbuf[:, ncol:ncol + 16], AF.Sqrt,
            bias=g["cb"][:, 2:3]),
            reads=("p4n", "cb"), writes=("p4nb",), tag=f"{pfx}sqrt{i}")
        sb.op("dve", lambda e: e.tensor_scalar_max(
            nbuf[:, ncol + 16:ncol + 32], nbuf[:, ncol + 16:ncol + 32], 1e-12),
            reads=("p4nb",), writes=("p4nb",), tag=f"{pfx}max{i}")
        sb.op("dve", lambda e: e.reciprocal(
            nbuf[:, ncol:ncol + 16], nbuf[:, ncol + 16:ncol + 32]),
            reads=("p4nb",), writes=("p4n",), tag=f"{pfx}rc{i}")

    for i in range(NT):
        p = i % 2
        sb.dma("qs", lambda e, i=i, p=p: e.dma_start(
            xt[p][:, :], x0f[i * 128:(i + 1) * 128, :]),
            writes=(f"p4x{p}",), tag=f"p4ld{i}")
        for j, (gi, k) in enumerate([(0, 0), (0, 1), (1, 0), (1, 1)]):
            sb.dma("qg", lambda e, i=i, j=j, gi=gi, k=k: e.indirect_dma_start(
                yg[j][:, :], None, g[f"ybuf{gi}"],
                bass.IndirectOffsetOnAxis(g[f"destI{gi}"][:, 2 * i + k:2 * i + k + 1], 0)),
                reads=(f"ybuf{gi}", f"destI{gi}"), writes=(f"p4yg{j}",),
                tag=f"p4g{i}_{j}")
        for j, (gi, k) in enumerate([(0, 0), (0, 1), (1, 0), (1, 1)]):
            sb.op("dve", lambda e, i=i, p=p, j=j, gi=gi, k=k: e.scalar_tensor_tensor(
                x1t[p][:, :], yg[j][:, :],
                g[f"Wbuf{gi}"][:, 2 * i + k:2 * i + k + 1],
                (xt[p] if j == 0 else x1t[p])[:, :], ALU.mult, ALU.add),
                reads=(f"p4yg{j}", f"Wbuf{gi}",
                       (f"p4x{p}" if j == 0 else f"p4x1{p}")),
                writes=(f"p4x1{p}",), tag=f"p4acc{i}_{j}")
        sb.dma("qs", lambda e, i=i, p=p: e.dma_start(
            g["xr"][i * 128:(i + 1) * 128, :], x1t[p][:, :]),
            reads=(f"p4x1{p}",), writes=("xr",), tag=f"p4xr{i}")
        _ln_tile(ctx, "p4ln", x1t[p][:, :], (f"p4x1{p}",), hbt[p][:, :],
                 f"p4hb{p}", smb[p], f"p4sm{p}", sqb[0][:, :], "p4sq",
                 0, i)
        ts = i % 4
        for kc in range(8):
            bk = kc % 2
            sb.op("pe", lambda e, p=p, kc=kc, bk=bk: e.transpose(
                psT[bk][:, :], hbt[p][:, kc * 128:(kc + 1) * 128],
                g["identB"][:, :]),
                reads=(f"p4hb{p}", "identB"), writes=(f"p4psT{bk}",),
                tag=f"p4tr{i}_{kc}")
            sb.op("dve", lambda e, kc=kc, bk=bk, ts=ts: e.tensor_copy(
                hBf[:, kc, ts * 128:(ts + 1) * 128], psT[bk][:, :]),
                reads=(f"p4psT{bk}",), writes=(f"p4hbf_{kc}",),
                tag=f"p4ev{i}_{kc}")
        if ts != 3:
            continue
        grp4 = i // 4
        for ts2 in range(4):
            ti = grp4 * 4 + ts2
            r0 = ti * 128
            for wi, wn in enumerate(("q", "k", "v")):
                pp = wi % 2
                for half in range(2):
                    for kc in range(8):
                        sb.op("pe", lambda e, wn=wn, ts2=ts2, half=half, kc=kc,
                              pp=pp: e.matmul(
                            psP[pp][:, half * 512:(half + 1) * 512],
                            hBf[:, kc, ts2 * 128:(ts2 + 1) * 128],
                            WT[wn][:, kc, half * 512:(half + 1) * 512],
                            start=(kc == 0), stop=(kc == 7)),
                            reads=(f"p4hbf_{kc}", f"p4wt{wn}"),
                            writes=(f"p4psP{pp}",), tag=f"p4p{wn}{ti}_{half}{kc}"),
                bias = {"q": "r_wqb", "k": "r_wkb", "v": "r_wvb"}[wn]
                dst = pf32 if wn != "v" else dts
                dres = "p4pf" if wn != "v" else "p4d"
                sb.op("dve", lambda e, pp=pp, bias=bias, dst=dst: e.tensor_add(
                    dst[:, :], psP[pp][:, :], g[bias][:, :]),
                    reads=(f"p4psP{pp}", bias), writes=(dres,),
                    tag=f"p4ev{wn}{ti}")
                if wn == "q":
                    seg_norm("p4nq", pf32[:, :], "p4pf", 0, ti)
                    sb.op("dve", lambda e: e.tensor_tensor(
                        qtb[:, :], pf32[:, :],
                        ap(nbuf, 0, [[8 * H, 128], [1, 16], [0, 64]]), ALU.mult),
                        reads=("p4pf", "p4n"), writes=("p4qtb",),
                        tag=f"p4qn{ti}")
                    sb.dma("qs", lambda e, r0=r0: e.dma_start(
                        g["qtd"][r0:r0 + 128, :], qtb[:, :]),
                        reads=("p4qtb",), writes=("qtd",), tag=f"p4qs{ti}")
                elif wn == "k":
                    seg_norm("p4nk", pf32[:, :], "p4pf", 0, ti)
                    sb.op("dve", lambda e: e.tensor_tensor(
                        kn[:, :], pf32[:, :],
                        ap(nbuf, 0, [[8 * H, 128], [1, 16], [0, 64]]), ALU.mult),
                        reads=("p4pf", "p4n"), writes=("p4kn",),
                        tag=f"p4kn{ti}")
                    sb.dma("qg", lambda e, r0=r0: e.dma_start(
                        g["ktd"][r0:r0 + 128, :], kn[:, :]),
                        reads=("p4kn",), writes=("ktd",), tag=f"p4ks{ti}")
                else:
                    sb.op("dve", lambda e: e.tensor_sub(
                        dts[:, :], dts[:, :], kn[:, :]),
                        reads=("p4d", "p4kn"), writes=("p4d",), tag=f"p4dv{ti}")
                    sb.op("dve", lambda e: e.reduce_sum(
                        nbuf[:, 32:48],
                        ap(dts, 0, [[D, 128], [64, 16], [1, 64]]), axis=AX.X),
                        reads=("p4d",), writes=("p4n2",), tag=f"p4dm{ti}")
                    sb.op("dve", lambda e: e.tensor_scalar(
                        nbuf[:, 32:48], nbuf[:, 32:48], 1.0 / 64, None, ALU.mult),
                        reads=("p4n2",), writes=("p4n2",), tag=f"p4dm2{ti}")
                    sb.op("act", lambda e: e.activation(
                        sqb[0][:, :], dts[:, :], AF.Square),
                        reads=("p4d",), writes=("p4sq",), tag=f"p4d2{ti}")
                    sb.op("dve", lambda e: e.reduce_sum(
                        nbuf[:, 48:64],
                        ap(sqb[0], 0, [[D, 128], [64, 16], [1, 64]]), axis=AX.X),
                        reads=("p4sq",), writes=("p4n3",), tag=f"p4e2{ti}")
                    sb.op("dve", lambda e: e.tensor_scalar(
                        nbuf[:, 48:64], nbuf[:, 48:64], 1.0 / 64, None, ALU.mult),
                        reads=("p4n3",), writes=("p4n3",), tag=f"p4e2b{ti}")
                    sb.op("dve", lambda e: e.tensor_mul(
                        nbuf[:, 64:80], nbuf[:, 32:48], nbuf[:, 32:48]),
                        reads=("p4n2",), writes=("p4n4",), tag=f"p4mm{ti}")
                    sb.op("dve", lambda e: e.tensor_sub(
                        nbuf[:, 64:80], nbuf[:, 48:64], nbuf[:, 64:80]),
                        reads=("p4n3", "p4n4"), writes=("p4n4",), tag=f"p4vr{ti}")
                    sb.op("dve", lambda e: e.tensor_scalar(
                        nbuf[:, 64:80], nbuf[:, 64:80], 64.0 / 63.0, None,
                        ALU.mult),
                        reads=("p4n4",), writes=("p4n4",), tag=f"p4v63{ti}")
                    sb.op("act", lambda e: e.activation(
                        nbuf[:, 80:96], nbuf[:, 64:80], AF.Sqrt,
                        bias=g["cb"][:, 2:3]),
                        reads=("p4n4", "cb"), writes=("p4n5",), tag=f"p4sd{ti}")
                    sb.op("dve", lambda e: e.tensor_scalar(
                        nbuf[:, 80:96], nbuf[:, 80:96], 1e-5, None, ALU.add),
                        reads=("p4n5",), writes=("p4n5",), tag=f"p4sde{ti}")
                    sb.op("dve", lambda e: e.reciprocal(
                        nbuf[:, 96:112], nbuf[:, 80:96]),
                        reads=("p4n5",), writes=("p4n6",), tag=f"p4rs{ti}")
                    sb.op("pool", lambda e: e.tensor_tensor(
                        t1[:, :],
                        ap(dts, 0, [[D, 128], [64, 16], [1, 64]]),
                        ap(nbuf, 32, [[8 * H, 128], [1, 16], [0, 64]]),
                        ALU.subtract),
                        reads=("p4d", "p4n2"), writes=("p4t1",), tag=f"p4t1{ti}")
                    sb.op("pool", lambda e: e.tensor_tensor(
                        t1[:, :],
                        ap(t1, 0, [[D, 128], [64, 16], [1, 64]]),
                        ap(nbuf, 96, [[8 * H, 128], [1, 16], [0, 64]]),
                        ALU.mult),
                        reads=("p4t1", "p4n6"), writes=("p4t1",), tag=f"p4t2{ti}")
                    sb.op("dve", lambda e: e.tensor_mul(
                        t1[:, :], t1[:, :], g["r_tnw"][:, :]),
                        reads=("p4t1", "r_tnw"), writes=("p4t1",), tag=f"p4t3{ti}")
                    sb.op("dve", lambda e: e.tensor_add(
                        qtb[:, :], t1[:, :], g["r_tnb"][:, :]),
                        reads=("p4t1", "r_tnb"), writes=("p4qtb",), tag=f"p4t4{ti}")
                    sb.dma("qs", lambda e, r0=r0: e.dma_start(
                        g["tgd"][r0:r0 + 128, :], qtb[:, :]),
                        reads=("p4qtb",), writes=("tgd",), tag=f"p4tg{ti}")
            for kc in range(8):
                sb.op("pe", lambda e, ts2=ts2, kc=kc: e.matmul(
                    psLr[:, :], hBf[:, kc, ts2 * 128:(ts2 + 1) * 128],
                    g["lrwT"][:, kc, :], start=(kc == 0), stop=(kc == 7)),
                    reads=(f"p4hbf_{kc}", "lrwT"), writes=("p4psLr",),
                    tag=f"p4lr{ti}_{kc}")
            sb.op("dve", lambda e: e.tensor_add(
                etb[:, :], psLr[:, :], g["r_lrb"][:, 0:H]),
                reads=("p4psLr", "r_lrb"), writes=("p4etb",), tag=f"p4le{ti}")
            sb.op("act", lambda e: e.activation(etb[:, :], etb[:, :], AF.Sigmoid),
                  reads=("p4etb",), writes=("p4etb",), tag=f"p4lsig{ti}")
            sb.op("dve", lambda e: e.tensor_scalar(
                etb[:, :], etb[:, :], 1.0 / (HD * C), None, ALU.mult),
                reads=("p4etb",), writes=("p4etb",), tag=f"p4lsc{ti}")
            sb.dma("qs", lambda e, r0=r0: e.dma_start(
                g["etad"][r0:r0 + 128, :], etb[:, :]),
                reads=("p4etb",), writes=("etad",), tag=f"p4es{ti}")


def _p5_scan(ctx):
    nc, sb, g = ctx["nc"], ctx["sb"], ctx["g"]
    ph = ctx["phase"]
    sbuf, psum = ctx["sbuf"], ctx["psum"]

    for nm in ("r_pnw", "r_pnb", "r_wob"):
        sbuf(nm, [128, D], F32, ph)
    _build_repls(ctx, [("r_pnw", "a_post_norm_w", D),
                       ("r_pnb", "a_post_norm_b", D), ("r_wob", "a_wo_b", D)])
    WTo = sbuf("p5wto", [128, 8, D], BF16, ph)
    sb.dma("qs", lambda e: e.dma_start(WTo[:, :, :], g["wT_o"]),
           reads=("wT_o_d",), writes=("p5wto",), tag="p5ldwo")

    Wst = sbuf("p5Wst", [64, 2 * H * 64], F32, ph)
    Wbf = sbuf("p5Wbf", [64, 2 * H * 64], BF16, ph)
    bst = sbuf("p5bst", [1, 2 * D], F32, ph)
    bbf = sbuf("p5bbf", [1, 2 * D], BF16, ph)
    for b in range(2):
        for hh in range(H):
            sb.dma("qs", lambda e, b=b, hh=hh: e.dma_start(
                Wst[0:64, (b * H + hh) * 64:(b * H + hh + 1) * 64],
                g["a_W1"][hh]),
                reads=("Wst",), writes=("Wst",), tag=f"p5wi{b}_{hh}")
        sb.dma("qs", lambda e, b=b: e.dma_start(
            bst[0:1, b * D:(b + 1) * D],
            g["a_b1"].rearrange("h o d -> (h o d)")),
            writes=("bst",), tag=f"p5bi{b}")
    sb.dma("qs", lambda e: e.dma_start(
        g["flushb"][:, :], g["c_tri"]),
        reads=("Wst", "bst"), writes=("Wst", "bst", "flushb"), tag="p5fence")
    sb.op("dve", lambda e: e.tensor_copy(Wbf[:, :], Wst[:, :]),
          reads=("Wst",), writes=("Wbf",), tag="p5wc")
    sb.op("dve", lambda e: e.tensor_copy(bbf[0:1, :], bst[0:1, :]),
          reads=("bst",), writes=("bbf",), tag="p5bc")
    sb.barrier("p5init")
    onesZ = [sbuf(f"p5oz{b}", [128, 1], BF16, ph) for b in range(2)]
    for b in range(2):
        sb.op("dve", lambda e, b=b: e.memset(onesZ[b][:, :], 0.0),
              writes=(f"onesZ{b}",), tag=f"p5oz{b}a")
        sb.op("dve", lambda e, b=b: e.memset(onesZ[b][b * 64:(b + 1) * 64, :], 1.0),
              reads=(f"onesZ{b}",), writes=(f"onesZ{b}",), tag=f"p5oz{b}b")

    Qt = [sbuf(f"p5qt{p}", [128, D], BF16, ph) for p in range(2)]
    Kt = [sbuf(f"p5kt{p}", [128, D], BF16, ph) for p in range(2)]
    TG = [sbuf(f"p5tg{p}", [128, D], BF16, ph) for p in range(2)]
    ET = [sbuf(f"p5et{p}", [128, H], F32, ph) for p in range(2)]
    X1 = [sbuf(f"p5x1{p}", [128, D], F32, ph) for p in range(2)]
    Qf = sbuf("p5qf", [64, H, 128], BF16, ph)
    Kf = sbuf("p5kf", [64, H, 128], BF16, ph)
    nEta = sbuf("p5ne", [128, H], F32, ph)
    exkZ = [sbuf(f"p5exkz{b}", [128, D], BF16, ph) for b in range(2)]
    A2z = [sbuf(f"p5a2z{b}", [128, D], BF16, ph) for b in range(2)]
    egB = sbuf("p5egb", [128, D], BF16, ph)
    for b in range(2):
        sb.op("dve", lambda e, b=b: e.memset(exkZ[b][:, :], 0.0),
              writes=(f"p5exk{b}",), tag=f"p5xz{b}")
        sb.op("dve", lambda e, b=b: e.memset(A2z[b][:, :], 0.0),
              writes=(f"p5a2z{b}",), tag=f"p5az{b}")
    bufA = sbuf("p5bA", [128, D], F32, ph)
    bufB = sbuf("p5bB", [128, D], F32, ph)
    bufC = sbuf("p5bC", [128, D], F32, ph)
    bufD = sbuf("p5bD", [128, D], F32, ph)
    bufE = sbuf("p5bE", [128, D], F32, ph)
    gradB = sbuf("p5gr", [128, D], BF16, ph)
    yoB = sbuf("p5yo", [128, D], BF16, ph)
    Yf = sbuf("p5yf", [128, 8, 128], BF16, ph)
    x2b = sbuf("p5x2", [128, D], F32, ph)
    hCt = sbuf("p5hc", [128, D], BF16, ph)
    hCf = sbuf("p5hcf", [128, 8, 128], BF16, ph)
    nb = sbuf("p5nb", [128, 8 * H], F32, ph)
    smb = sbuf("p5sm", [128, 16], F32, ph)
    exps = sbuf("p5exps", [128, 2 * NE], F32, ph)
    scores = sbuf("p5scores", [128, 2 * NE], F32, ph)
    m8 = sbuf("p5m8", [128, 16], F32, ph)
    idx8 = sbuf("p5idx8", [128, 16], U32, ph)

    psZ = psum("p5psZ", [128, D], F32, ph)
    psQK = psum("p5psQK", [128, D], F32, ph)
    psT = [psum("p5psTA", [128, 128], BF16, ph),
           psum("p5psTB", [128, 128], BF16, ph)]
    psMix = [psum(f"p5psM{p}", [128, 512], F32, ph) for p in range(2)]

    import os
    nsteps = int(os.environ.get("P5_STEPS", str(NC)))
    SUB = int(os.environ.get("P5_SUB", "99"))
    for n in range(nsteps):
        p = n % 2
        sfx = f"_{n}"
        for nmr, buf, res, w in (("qtd", Qt[p], f"p5qt{p}", D),
                                 ("ktd", Kt[p], f"p5kt{p}", D),
                                 ("tgd", TG[p], f"p5tg{p}", D),
                                 ("etad", ET[p], f"p5et{p}", H),
                                 ("xr", X1[p], f"p5x1{p}", D)):
            sb.dma("qs", lambda e, nmr=nmr, buf=buf, n=n, w=w:
                   e.dma_start(buf[:, 0:w],
                               g[nmr].rearrange("(b l) d -> b l d", b=2)
                               [:, n * 64:(n + 1) * 64, 0:w]),
                   reads=(nmr,), writes=(res,), tag=f"p5ld{nmr}{sfx}")
        if n == 0 and os.environ.get("P5_TAPQ"):
            dram = ctx["dram"]
            for nm2, buf2, res2 in (("dbgq", Qt[p], f"p5qt{p}"),
                                    ("dbgk", Kt[p], f"p5kt{p}"),
                                    ("dbgt", TG[p], f"p5tg{p}")):
                tb = dram(nm2, [128, D], BF16, "ExternalOutput")
                sb.dma("qg", lambda e, tb=tb, buf2=buf2: e.dma_start(tb, buf2[:, :]),
                       reads=(res2,), writes=(nm2,), tag="tap" + nm2)
        sb.op("dve", lambda e, p=p: e.tensor_scalar(
            nEta[:, :], ET[p][:, :], -1.0, None, ALU.mult),
            reads=(f"p5et{p}",), writes=("p5ne",), tag=f"p5neg{sfx}")
        for b in range(2):
            sb.op("dve", lambda e, p=p, b=b: e.tensor_tensor(
                ap(exkZ[b], b * 64 * D, [[D, 64], [64, H], [1, 64]]),
                ap(Kt[p], b * 64 * D, [[D, 64], [64, H], [1, 64]]),
                ap(ET[p], b * 64 * H, [[H, 64], [1, H], [0, 64]]), ALU.mult),
                reads=(f"p5kt{p}", f"p5et{p}"), writes=(f"p5exk{b}",),
                tag=f"p5exk{b}{sfx}")
        for srcb, dstf, rs, res in ((Qt[p], Qf, f"p5qt{p}", "p5qf"),
                                    (Kt[p], Kf, f"p5kt{p}", "p5kf")):
            for hh in range(H):
                bk = hh % 2
                sb.op("pe", lambda e, srcb=srcb, hh=hh, bk=bk: e.transpose(
                    psT[bk][0:64, :], srcb[:, hh * 64:(hh + 1) * 64],
                    g["identB"][:, :]),
                    reads=(rs, "identB"), writes=(f"p5psT{bk}",),
                    tag=f"p5tr{res}{hh}{sfx}")
                sb.op("dve", lambda e, dstf=dstf, hh=hh, bk=bk, res=res:
                      e.tensor_copy(dstf[0:64, hh, :], psT[bk][0:64, :]),
                      reads=(f"p5psT{bk}",), writes=(res, f"{res}_{hh}"),
                      tag=f"p5te{res}{hh}{sfx}")
        if SUB < 2:
            continue
        for hh in range(H):
            for b in range(2):
                sb.op("pe", lambda e, hh=hh, b=b: e.matmul(
                    psZ[b * 64:(b + 1) * 64, hh * 64:(hh + 1) * 64],
                    Kf[0:64, hh, b * 64:(b + 1) * 64],
                    Wbf[0:64, (b * H + hh) * 64:(b * H + hh + 1) * 64],
                    start=(hh in (0, 8)), stop=True,
                    tile_position=(0, b * 64)),
                    reads=(f"p5kf_{hh}", "Wbf"), writes=("psZ",),
                    tag=f"p5z{hh}{b}{sfx}")
        for b in range(2):
            for pr in range(8):
                sb.op("pe", lambda e, pr=pr, b=b: e.matmul(
                    psZ[b * 64:(b + 1) * 64, pr * 128:(pr + 1) * 128],
                    g["onesB"][0:1, 0:64],
                    bbf[0:1, b * D + pr * 128:b * D + (pr + 1) * 128],
                    start=False, stop=True, tile_position=(0, b * 64)),
                    reads=("onesB", "bbf"), writes=("psZ",),
                    tag=f"p5zb{pr}{b}{sfx}")
        if SUB < 3:
            continue
        for hh in range(H):
            for b in range(2):
                sb.op("pe", lambda e, hh=hh, b=b: e.matmul(
                    psQK[b * 64:(b + 1) * 64, hh * 64:(hh + 1) * 64],
                    Kf[0:64, hh, b * 64:(b + 1) * 64],
                    Qf[0:64, hh, b * 64:(b + 1) * 64],
                    start=(hh in (0, 8)), stop=True,
                    tile_position=(0, b * 64)),
                    reads=(f"p5kf_{hh}", f"p5qf_{hh}"), writes=("psQK",),
                    tag=f"p5qk{hh}{b}{sfx}")
        if SUB < 4:
            continue
        sb.op("dve", lambda e: e.tensor_copy(bufA[:, :], psZ[:, :]),
              reads=("psZ",), writes=("p5bA",), tag=f"p5zev{sfx}")
        sb.op("dve", lambda e: e.reduce_sum(
            nb[:, 0:16], ap(bufA, 0, [[D, 128], [64, H], [1, 64]]), axis=AX.X),
            reads=("p5bA",), writes=("p5nb0",), tag=f"p5mu{sfx}")
        sb.op("dve", lambda e: e.tensor_scalar(
            nb[:, 0:16], nb[:, 0:16], 1.0 / 64, None, ALU.mult),
            reads=("p5nb0",), writes=("p5nb0",), tag=f"p5mu2{sfx}")
        sb.op("act", lambda e: e.activation(bufD[:, :], bufA[:, :], AF.Square),
              reads=("p5bA",), writes=("p5bD",), tag=f"p5z2{sfx}")
        sb.op("dve", lambda e: e.reduce_sum(
            nb[:, 16:32], ap(bufD, 0, [[D, 128], [64, H], [1, 64]]), axis=AX.X),
            reads=("p5bD",), writes=("p5nb1",), tag=f"p5e2{sfx}")
        sb.op("dve", lambda e: e.tensor_scalar(
            nb[:, 16:32], nb[:, 16:32], 1.0 / 64, None, ALU.mult),
            reads=("p5nb1",), writes=("p5nb1",), tag=f"p5e2b{sfx}")
        sb.op("dve", lambda e: e.tensor_mul(nb[:, 32:48], nb[:, 0:16], nb[:, 0:16]),
              reads=("p5nb0",), writes=("p5nb2",), tag=f"p5m2{sfx}")
        sb.op("dve", lambda e: e.tensor_sub(nb[:, 32:48], nb[:, 16:32], nb[:, 32:48]),
              reads=("p5nb1", "p5nb2"), writes=("p5nb2",), tag=f"p5var{sfx}")
        sb.op("act", lambda e: e.activation(
            nb[:, 48:64], nb[:, 32:48], AF.Sqrt, bias=g["cb"][:, 1:2]),
            reads=("p5nb2", "cb"), writes=("p5nb3",), tag=f"p5std{sfx}")
        sb.op("dve", lambda e: e.reciprocal(nb[:, 32:48], nb[:, 48:64]),
              reads=("p5nb3",), writes=("p5nb2",), tag=f"p5rst{sfx}")
        sb.op("pool", lambda e: e.tensor_tensor(
            bufB[:, :], ap(bufA, 0, [[D, 128], [64, H], [1, 64]]),
            ap(nb, 0, [[8 * H, 128], [1, H], [0, 64]]), ALU.subtract),
            reads=("p5bA", "p5nb0"), writes=("p5bB",), tag=f"p5xh1{sfx}")
        sb.op("dve", lambda e: e.tensor_tensor(
            bufB[:, :], ap(bufB, 0, [[D, 128], [64, H], [1, 64]]),
            ap(nb, 32, [[8 * H, 128], [1, H], [0, 64]]), ALU.mult),
            reads=("p5bB", "p5nb2"), writes=("p5bB",), tag=f"p5xh2{sfx}")
        sb.op("pool", lambda e, p=p: e.tensor_sub(
            bufE[:, :], g["r_tnb"][:, :], TG[p][:, :]),
            reads=("r_tnb", f"p5tg{p}"), writes=("p5bE",), tag=f"p5tg2{sfx}")
        sb.op("dve", lambda e: e.tensor_mul(bufC[:, :], bufB[:, :], g["r_tnw"][:, :]),
              reads=("p5bB", "r_tnw"), writes=("p5bC",), tag=f"p5go1{sfx}")
        sb.op("dve", lambda e: e.tensor_add(bufC[:, :], bufC[:, :], bufE[:, :]),
              reads=("p5bC", "p5bE"), writes=("p5bC",), tag=f"p5go2{sfx}")
        sb.op("dve", lambda e: e.tensor_mul(bufC[:, :], bufC[:, :], g["r_tnw"][:, :]),
              reads=("p5bC", "r_tnw"), writes=("p5bC",), tag=f"p5gx{sfx}")
        sb.op("dve", lambda e: e.reduce_sum(
            nb[:, 48:64], ap(bufC, 0, [[D, 128], [64, H], [1, 64]]), axis=AX.X),
            reads=("p5bC",), writes=("p5nb3",), tag=f"p5s1{sfx}")
        sb.op("pool", lambda e: e.tensor_mul(bufD[:, :], bufC[:, :], bufB[:, :]),
              reads=("p5bC", "p5bB"), writes=("p5bD",), tag=f"p5gx2{sfx}")
        sb.op("dve", lambda e: e.reduce_sum(
            nb[:, 64:80], ap(bufD, 0, [[D, 128], [64, H], [1, 64]]), axis=AX.X),
            reads=("p5bD",), writes=("p5nb4",), tag=f"p5s2{sfx}")
        sb.op("dve", lambda e: e.tensor_scalar(
            bufD[:, :], bufC[:, :], 64.0, None, ALU.mult),
            reads=("p5bC",), writes=("p5bD",), tag=f"p5g64{sfx}")
        sb.op("pool", lambda e: e.tensor_tensor(
            bufD[:, :], ap(bufD, 0, [[D, 128], [64, H], [1, 64]]),
            ap(nb, 48, [[8 * H, 128], [1, H], [0, 64]]), ALU.subtract),
            reads=("p5bD", "p5nb3"), writes=("p5bD",), tag=f"p5gs1{sfx}")
        sb.op("dve", lambda e: e.tensor_tensor(
            bufC[:, :], ap(bufB, 0, [[D, 128], [64, H], [1, 64]]),
            ap(nb, 64, [[8 * H, 128], [1, H], [0, 64]]), ALU.mult),
            reads=("p5bB", "p5nb4"), writes=("p5bC",), tag=f"p5gs2{sfx}")
        sb.op("dve", lambda e: e.tensor_sub(bufD[:, :], bufD[:, :], bufC[:, :]),
              reads=("p5bD", "p5bC"), writes=("p5bD",), tag=f"p5gd{sfx}")
        sb.op("dve", lambda e: e.tensor_scalar(
            nb[:, 48:64], nb[:, 32:48], 1.0 / 64, None, ALU.mult),
            reads=("p5nb2",), writes=("p5nb3",), tag=f"p5r64{sfx}")
        sb.op("dve", lambda e: e.tensor_tensor(
            gradB[:, :], ap(bufD, 0, [[D, 128], [64, H], [1, 64]]),
            ap(nb, 48, [[8 * H, 128], [1, H], [0, 64]]), ALU.mult),
            reads=("p5bD", "p5nb3"), writes=("p5gr",), tag=f"p5grad{sfx}")
        if SUB < 5:
            continue
        sb.op("dve", lambda e: e.tensor_scalar(
            bufD[:, :], psQK[:, :], 1.0, None, ALU.add),
            reads=("psQK",), writes=("p5bD",), tag=f"p5a1{sfx}")
        sb.op("dve", lambda e: e.tensor_tensor(
            bufD[:, :], ap(bufD, 0, [[D, 128], [64, H], [1, 64]]),
            ap(nEta, 0, [[H, 128], [1, H], [0, 64]]), ALU.mult),
            reads=("p5bD", "p5ne"), writes=("p5bD",), tag=f"p5a2{sfx}")
        for b in range(2):
            sb.op("dve", lambda e, b=b: e.tensor_tensor(
                ap(A2z[b], b * 64 * D, [[D, 64], [64, H], [1, 64]]),
                ap(bufD, b * 64 * D, [[D, 64], [64, H], [1, 64]]),
                ap(g["triF"], b * 64 * 64, [[64, 64], [0, H], [1, 64]]),
                ALU.mult),
                reads=("p5bD", "triF"), writes=(f"p5a2z{b}",),
                tag=f"p5a3{b}{sfx}")
        if SUB < 6:
            continue
        for hh in range(H):
            for b in range(2):
                sb.op("pe", lambda e, hh=hh, b=b: e.matmul(
                    psZ[b * 64:(b + 1) * 64, hh * 64:(hh + 1) * 64],
                    Qf[0:64, hh, b * 64:(b + 1) * 64],
                    Wbf[0:64, (b * H + hh) * 64:(b * H + hh + 1) * 64],
                    start=(hh in (0, 8)), stop=True,
                    tile_position=(0, b * 64)),
                    reads=(f"p5qf_{hh}", "Wbf", "psZ"), writes=("psZ",),
                    tag=f"p5zb1{hh}{b}{sfx}")
        for b in range(2):
            for pr in range(8):
                sb.op("pe", lambda e, pr=pr, b=b: e.matmul(
                    psZ[b * 64:(b + 1) * 64, pr * 128:(pr + 1) * 128],
                    g["onesB"][0:1, 0:64],
                    bbf[0:1, b * D + pr * 128:b * D + (pr + 1) * 128],
                    start=False, stop=True, tile_position=(0, b * 64)),
                    reads=("onesB", "bbf"), writes=("psZ",),
                    tag=f"p5zb2{pr}{b}{sfx}")
        for hh in range(H):
            for b in range(2):
                sb.op("pe", lambda e, hh=hh, b=b: e.matmul(
                    psZ[b * 64:(b + 1) * 64, hh * 64:(hh + 1) * 64],
                    A2z[b][:, hh * 64:(hh + 1) * 64],
                    gradB[:, hh * 64:(hh + 1) * 64],
                    start=False, stop=True, tile_position=(0, b * 64)),
                    reads=(f"p5a2z{b}", "p5gr"), writes=("psZ",),
                    tag=f"p5zb3{hh}{b}{sfx}")
        if SUB < 7:
            continue
        for hh in range(H):
            mp = hh % 2
            for b in range(2):
                sb.op("pe", lambda e, hh=hh, b=b, mp=mp: e.matmul(
                    psMix[mp][0:64, b * 64:(b + 1) * 64],
                    exkZ[b][:, hh * 64:(hh + 1) * 64],
                    gradB[:, hh * 64:(hh + 1) * 64],
                    start=True, stop=True, tile_position=(0, 0)),
                    reads=(f"p5exk{b}", "p5gr"), writes=(f"psM{mp}",),
                    tag=f"p5u{hh}{b}{sfx}")
            for b in range(2):
                sb.op("dve", lambda e, hh=hh, b=b, mp=mp: e.tensor_sub(
                    Wst[0:64, (b * H + hh) * 64:(b * H + hh + 1) * 64],
                    Wst[0:64, (b * H + hh) * 64:(b * H + hh + 1) * 64],
                    psMix[mp][0:64, b * 64:(b + 1) * 64]),
                    reads=(f"psM{mp}", "Wst"), writes=("Wst",),
                    tag=f"p5ws{hh}{b}{sfx}")
        sb.op("dve", lambda e: e.tensor_copy(Wbf[:, :], Wst[:, :]),
              reads=("Wst",), writes=("Wbf",), tag=f"p5wr{sfx}")
        if SUB < 8:
            continue
        sb.op("dve", lambda e: e.tensor_copy(bufA[:, :], psZ[:, :]),
              reads=("psZ",), writes=("p5bA",), tag=f"p5ze2{sfx}")
        sb.op("dve", lambda e: e.reduce_sum(
            nb[:, 0:16], ap(bufA, 0, [[D, 128], [64, H], [1, 64]]), axis=AX.X),
            reads=("p5bA",), writes=("p5nb0",), tag=f"p5ymu{sfx}")
        sb.op("dve", lambda e: e.tensor_scalar(
            nb[:, 0:16], nb[:, 0:16], 1.0 / 64, None, ALU.mult),
            reads=("p5nb0",), writes=("p5nb0",), tag=f"p5ymu2{sfx}")
        sb.op("act", lambda e: e.activation(bufD[:, :], bufA[:, :], AF.Square),
              reads=("p5bA",), writes=("p5bD",), tag=f"p5y2{sfx}")
        sb.op("dve", lambda e: e.reduce_sum(
            nb[:, 16:32], ap(bufD, 0, [[D, 128], [64, H], [1, 64]]), axis=AX.X),
            reads=("p5bD",), writes=("p5nb1",), tag=f"p5ye2{sfx}")
        sb.op("dve", lambda e: e.tensor_scalar(
            nb[:, 16:32], nb[:, 16:32], 1.0 / 64, None, ALU.mult),
            reads=("p5nb1",), writes=("p5nb1",), tag=f"p5ye2b{sfx}")
        sb.op("dve", lambda e: e.tensor_mul(nb[:, 32:48], nb[:, 0:16], nb[:, 0:16]),
              reads=("p5nb0",), writes=("p5nb2",), tag=f"p5ym2{sfx}")
        sb.op("dve", lambda e: e.tensor_sub(nb[:, 32:48], nb[:, 16:32], nb[:, 32:48]),
              reads=("p5nb1", "p5nb2"), writes=("p5nb2",), tag=f"p5yvar{sfx}")
        sb.op("act", lambda e: e.activation(
            nb[:, 48:64], nb[:, 32:48], AF.Sqrt, bias=g["cb"][:, 1:2]),
            reads=("p5nb2", "cb"), writes=("p5nb3",), tag=f"p5ystd{sfx}")
        sb.op("dve", lambda e: e.reciprocal(nb[:, 32:48], nb[:, 48:64]),
              reads=("p5nb3",), writes=("p5nb2",), tag=f"p5yrst{sfx}")
        sb.op("pool", lambda e: e.tensor_tensor(
            bufB[:, :], ap(bufA, 0, [[D, 128], [64, H], [1, 64]]),
            ap(nb, 0, [[8 * H, 128], [1, H], [0, 64]]), ALU.subtract),
            reads=("p5bA", "p5nb0"), writes=("p5bB",), tag=f"p5yx1{sfx}")
        sb.op("dve", lambda e: e.tensor_tensor(
            bufB[:, :], ap(bufB, 0, [[D, 128], [64, H], [1, 64]]),
            ap(nb, 32, [[8 * H, 128], [1, H], [0, 64]]), ALU.mult),
            reads=("p5bB", "p5nb2"), writes=("p5bB",), tag=f"p5yx2{sfx}")
        sb.op("dve", lambda e: e.tensor_mul(bufB[:, :], bufB[:, :], g["r_tnw"][:, :]),
              reads=("p5bB", "r_tnw"), writes=("p5bB",), tag=f"p5yg{sfx}")
        sb.op("dve", lambda e: e.tensor_add(bufB[:, :], bufB[:, :], g["r_tnb"][:, :]),
              reads=("p5bB", "r_tnb"), writes=("p5bB",), tag=f"p5yb{sfx}")
        sb.op("dve", lambda e, p=p: e.tensor_add(bufB[:, :], bufB[:, :], Qt[p][:, :]),
              reads=("p5bB", f"p5qt{p}"), writes=("p5bB",), tag=f"p5yq{sfx}")
        if SUB < 9:
            continue
        sb.op("dve", lambda e, p=p: e.tensor_tensor(
            ap(egB, 0, [[D, 128], [64, H], [1, 64]]),
            ap(gradB, 0, [[D, 128], [64, H], [1, 64]]),
            ap(ET[p], 0, [[H, 128], [1, H], [0, 64]]), ALU.mult),
            reads=("p5gr", f"p5et{p}"), writes=("p5egb",), tag=f"p5eg{sfx}")
        for b in range(2):
            for half in range(2):
                sb.op("pe", lambda e, b=b, half=half: e.matmul(
                    psZ[0:1, half * 512:(half + 1) * 512],
                    onesZ[b][:, 0:1],
                    egB[:, half * 512:(half + 1) * 512],
                    start=True, stop=True, tile_position=(0, 0)),
                    reads=(f"onesZ{b}", "p5egb"), writes=("psZ",),
                    tag=f"p5bu{b}{half}{sfx}")
            for half in range(2):
                sb.op("dve", lambda e, b=b, half=half: e.tensor_sub(
                    bst[0:1, b * D + half * 512:b * D + (half + 1) * 512],
                    bst[0:1, b * D + half * 512:b * D + (half + 1) * 512],
                    psZ[0:1, half * 512:(half + 1) * 512]),
                    reads=("psZ", "bst"), writes=("bst",),
                    tag=f"p5bs{b}{half}{sfx}")
        sb.op("dve", lambda e: e.tensor_copy(bbf[0:1, :], bst[0:1, :]),
              reads=("bst",), writes=("bbf",), tag=f"p5br{sfx}")
        if SUB < 10:
            continue
        sb.op("dve", lambda e: e.reduce_sum(smb[:, 0:1], bufB[:, :], axis=AX.X,
                                            negate=True),
              reads=("p5bB",), writes=("p5sm",), tag=f"p5ps1{sfx}")
        sb.op("act", lambda e: e.activation(bufD[:, :], bufB[:, :], AF.Square,
                                            accum_out=smb[:, 1:2]),
              reads=("p5bB",), writes=("p5bD", "p5smb"), tag=f"p5psq{sfx}")
        sb.op("dve", lambda e: e.tensor_scalar(smb[:, 2:3], smb[:, 0:1], 1.0 / D,
                                               None, ALU.mult),
              reads=("p5sm",), writes=("p5sm",), tag=f"p5pmu{sfx}")
        sb.op("dve", lambda e: e.tensor_scalar(smb[:, 3:4], smb[:, 1:2], 1.0 / D,
                                               None, ALU.mult),
              reads=("p5smb", "p5sm"), writes=("p5sm",), tag=f"p5pe2{sfx}")
        sb.op("dve", lambda e: e.tensor_scalar(smb[:, 4:5], smb[:, 2:3], smb[:, 2:3],
                                               None, ALU.mult),
              reads=("p5sm",), writes=("p5sm",), tag=f"p5pm2{sfx}")
        sb.op("dve", lambda e: e.tensor_sub(smb[:, 5:6], smb[:, 3:4], smb[:, 4:5]),
              reads=("p5sm",), writes=("p5sm",), tag=f"p5pvar{sfx}")
        sb.op("act", lambda e: e.activation(smb[:, 6:7], smb[:, 5:6], AF.Sqrt,
                                            bias=g["cb"][:, 1:2]),
              reads=("p5sm", "cb"), writes=("p5smc",), tag=f"p5pstd{sfx}")
        sb.op("dve", lambda e: e.reciprocal(smb[:, 7:8], smb[:, 6:7]),
              reads=("p5smc",), writes=("p5sm",), tag=f"p5prst{sfx}")
        sb.op("dve", lambda e: e.tensor_scalar(
            bufC[:, :], bufB[:, :], smb[:, 2:3], smb[:, 7:8], ALU.add, ALU.mult),
            reads=("p5bB", "p5sm"), writes=("p5bC",), tag=f"p5pap{sfx}")
        sb.op("pool", lambda e: e.tensor_mul(bufC[:, :], bufC[:, :],
                                             g["r_pnw"][:, :]),
              reads=("p5bC", "r_pnw"), writes=("p5bC",), tag=f"p5pw{sfx}")
        sb.op("dve", lambda e: e.tensor_add(yoB[:, :], bufC[:, :], g["r_pnb"][:, :]),
              reads=("p5bC", "r_pnb"), writes=("p5yo",), tag=f"p5pb{sfx}")
        if SUB < 11:
            continue
        for pr in range(8):
            bk = pr % 2
            sb.op("pe", lambda e, pr=pr, bk=bk: e.transpose(
                psT[bk][:, :], yoB[:, pr * 128:(pr + 1) * 128], g["identB"][:, :]),
                reads=("p5yo", "identB"), writes=(f"p5psT{bk}",),
                tag=f"p5ytr{pr}{sfx}")
            sb.op("dve", lambda e, pr=pr, bk=bk: e.tensor_copy(
                Yf[:, pr, :], psT[bk][:, :]),
                reads=(f"p5psT{bk}",), writes=(f"p5yf_{pr}",),
                tag=f"p5yte{pr}{sfx}")
        for half in range(2):
            for kc in range(8):
                sb.op("pe", lambda e, half=half, kc=kc: e.matmul(
                    psQK[:, half * 512:(half + 1) * 512],
                    Yf[:, kc, :], WTo[:, kc, half * 512:(half + 1) * 512],
                    start=(kc == 0), stop=(kc == 7)),
                    reads=(f"p5yf_{kc}", "p5wto", "psQK"), writes=("psQK",),
                    tag=f"p5o{half}{kc}{sfx}")
        sb.op("dve", lambda e: e.tensor_add(bufD[:, :], psQK[:, :], g["r_wob"][:, :]),
              reads=("psQK", "r_wob"), writes=("p5bD",), tag=f"p5oe{sfx}")
        sb.op("dve", lambda e, p=p: e.tensor_add(x2b[:, :], bufD[:, :], X1[p][:, :]),
              reads=("p5bD", f"p5x1{p}"), writes=("p5x2",), tag=f"p5x2{sfx}")
        sb.dma("qs", lambda e, n=n: e.dma_start(
            g["xr"].rearrange("(b l) d -> b l d", b=2)[:, n * 64:(n + 1) * 64, :],
            x2b[:, :]),
            reads=("p5x2",), writes=("xr",), tag=f"p5xw{sfx}")
        if SUB < 12:
            continue
        _ln_tile(ctx, "p5cln", x2b[:, :], ("p5x2",), hCt[:, :], "p5hct",
                 smb, "p5sm", bufD[:, :], "p5bD", 0, n)
        sb.op("dve", lambda e, n=n: e.tensor_copy(g["Hbuf"][:, n, :], hCt[:, :]),
              reads=("p5hct",), writes=("Hbuf",), tag=f"p5hb{sfx}")
        for kc in range(8):
            bk = kc % 2
            sb.op("pe", lambda e, kc=kc, bk=bk: e.transpose(
                psT[bk][:, :], hCt[:, kc * 128:(kc + 1) * 128], g["identB"][:, :]),
                reads=("p5hct", "identB"), writes=(f"p5psT{bk}",),
                tag=f"p5ctr{kc}{sfx}")
            sb.op("dve", lambda e, kc=kc, bk=bk: e.tensor_copy(
                hCf[:, kc, :], psT[bk][:, :]),
                reads=(f"p5psT{bk}",), writes=(f"p5hcf_{kc}",),
                tag=f"p5cte{kc}{sfx}")
        for gi in range(2):
            for kc in range(8):
                sb.op("pe", lambda e, gi=gi, kc=kc: e.matmul(
                    psMix[0][:, 256 + gi * NE:256 + (gi + 1) * NE],
                    hCf[:, kc, :], g[f"routerB{gi}"][:, kc, :],
                    start=(kc == 0 and gi == 0), stop=(kc == 7)),
                    reads=(f"p5hcf_{kc}", f"routerB{gi}"), writes=("psM0",),
                    tag=f"p5cl{gi}{kc}{sfx}")
        for gi in range(2):
            sb.op("dve", lambda e, gi=gi: e.reduce_max(
                smb[:, 8 + gi:9 + gi],
                psMix[0][:, 256 + gi * NE:256 + (gi + 1) * NE],
                axis=AX.X, negate=True),
                reads=("psM0",), writes=("p5sm",), tag=f"p5cnm{gi}{sfx}")
            sb.op("act", lambda e, gi=gi: e.activation(
                exps[:, gi * NE:(gi + 1) * NE],
                psMix[0][:, 256 + gi * NE:256 + (gi + 1) * NE],
                AF.Exp, bias=smb[:, 8 + gi:9 + gi],
                accum_out=smb[:, 10 + gi:11 + gi]),
                reads=("psM0", "p5sm"), writes=("p5exps", "p5smd"),
                tag=f"p5cex{gi}{sfx}")
            sb.op("dve", lambda e, gi=gi: e.reciprocal(
                smb[:, 12 + gi:13 + gi], smb[:, 10 + gi:11 + gi]),
                reads=("p5smd",), writes=("p5sm",), tag=f"p5crc{gi}{sfx}")
            sb.op("dve", lambda e, gi=gi: e.tensor_scalar(
                scores[:, gi * NE:(gi + 1) * NE], exps[:, gi * NE:(gi + 1) * NE],
                smb[:, 12 + gi:13 + gi], None, ALU.mult),
                reads=("p5exps", "p5sm"), writes=("p5scores",),
                tag=f"p5csc{gi}{sfx}")
            sb.op("dve", lambda e, gi=gi: e.max(
                out=m8[:, gi * 8:(gi + 1) * 8],
                in_=scores[:, gi * NE:(gi + 1) * NE]),
                reads=("p5scores",), writes=("p5m8",), tag=f"p5cm8{gi}{sfx}")
            sb.op("dve", lambda e, gi=gi: e.max_index(
                out=idx8[:, gi * 8:(gi + 1) * 8],
                in_max=m8[:, gi * 8:(gi + 1) * 8],
                in_values=scores[:, gi * NE:(gi + 1) * NE]),
                reads=("p5scores", "p5m8"), writes=("p5idx8",),
                tag=f"p5cmi{gi}{sfx}")
            sb.op("dve", lambda e, gi=gi, n=n: e.tensor_copy(
                g[f"Ebuf{gi}"][:, 2 * n:2 * n + 2], idx8[:, gi * 8:gi * 8 + 2]),
                reads=("p5idx8",), writes=(f"Ebuf{gi}",), tag=f"p5cec{gi}{sfx}")
            sb.op("dve", lambda e, gi=gi, n=n: e.tensor_scalar(
                g[f"Wbuf{gi}"][:, 2 * n:2 * n + 2], m8[:, gi * 8:gi * 8 + 2],
                0.25, None, ALU.mult),
                reads=("p5m8",), writes=(f"Wbuf{gi}",), tag=f"p5cwc{gi}{sfx}")


def _p8_output(ctx):
    nc, sb, g = ctx["nc"], ctx["sb"], ctx["g"]
    ph = ctx["phase"]
    sbuf = ctx["sbuf"]
    xt = [sbuf(f"p8x{p}", [128, D], F32, ph) for p in range(2)]
    yg = [sbuf(f"p8yg{j}", [128, D], BF16, ph) for j in range(4)]
    ot = [sbuf(f"p8o{p}", [128, D], F32, ph) for p in range(2)]
    outf = g["out"].rearrange("b l d -> (b l) d")
    for n in range(NC):
        p = n % 2
        sb.dma("qs", lambda e, n=n, p=p: e.dma_start(
            xt[p][:, :],
            g["xr"].rearrange("(b l) d -> b l d", b=2)[:, n * 64:(n + 1) * 64, :]),
            reads=("xr",), writes=(f"p8x{p}",), tag=f"p8ld_{n}")
        for j, (gi, k) in enumerate([(0, 0), (0, 1), (1, 0), (1, 1)]):
            sb.dma("qg", lambda e, n=n, j=j, gi=gi, k=k: e.indirect_dma_start(
                yg[j][:, :], None, g[f"ybuf{gi}"],
                bass.IndirectOffsetOnAxis(
                    g[f"destI{gi}"][:, 2 * n + k:2 * n + k + 1], 0)),
                reads=(f"ybuf{gi}", f"destI{gi}"), writes=(f"p8yg{j}",),
                tag=f"p8g{n}_{j}")
        for j, (gi, k) in enumerate([(0, 0), (0, 1), (1, 0), (1, 1)]):
            sb.op("dve", lambda e, n=n, p=p, j=j, gi=gi, k=k:
                  e.scalar_tensor_tensor(
                ot[p][:, :], yg[j][:, :],
                g[f"Wbuf{gi}"][:, 2 * n + k:2 * n + k + 1],
                (xt[p] if j == 0 else ot[p])[:, :], ALU.mult, ALU.add),
                reads=(f"p8yg{j}", f"Wbuf{gi}",
                       (f"p8x{p}" if j == 0 else f"p8o{p}")),
                writes=(f"p8o{p}",), tag=f"p8acc{n}_{j}")
        sb.dma("qs", lambda e, n=n, p=p: e.dma_start(
            outf.rearrange("(b l) d -> b l d", b=2)[:, n * 64:(n + 1) * 64, :],
            ot[p][:, :]),
            reads=(f"p8o{p}",), writes=("out",), tag=f"p8st_{n}")
